# version 18
# baseline (speedup 1.0000x reference)
import math
from contextlib import ExitStack
from functools import reduce

import numpy as np
import ml_dtypes

import concourse.bass as bass
import concourse.mybir as mybir
from concourse.bass_utils import run_bass_kernel_spmd

F32 = mybir.dt.float32
BF16 = mybir.dt.bfloat16
AF = mybir.ActivationFunctionType
ALU = mybir.AluOpType
BFD = ml_dtypes.bfloat16

B, T, C, NH, HD = 4, 2048, 2048, 16, 128
CCN = 16
TQ = 1024
KV = 2048
NKT = KV // 512
NTQ = TQ // 512
Q0P = 1024
DH = 8192
NHT = DH // 128
NOT_ = C // 128
NG = 8
G = 2

C_QS = float(np.float32(1.0 / math.sqrt(HD)))
C_B01 = float(np.float32(0.1))
C_EA = float(np.float32(1e-6))
C_EL = float(np.float32(1e-5))
C_IC = float(np.float32(1.0 / C))
C_Z = 0.0
C_N1 = -1.0
C_NH = -0.5
C_N5 = -5.0
C_N6 = -6.0
CONST_VALS = [C_QS, C_B01, C_EA, C_EL, C_IC, C_Z, C_N1, C_NH, C_N5, C_N6]
NCONST = 18

DEBUG = False


class Arena:

    def __init__(self, nc, name="arena"):
        self.nc = nc
        self.name = name
        self.size = 0
        self.tensors = []

    def at(self, offset, name, shape, dtype):
        per_part = reduce(lambda a, b: a * b, shape[1:]) * mybir.dt.size(dtype)
        assert offset % 32 == 0, (name, offset)
        self.size = max(self.size, offset + per_part)
        h = self.nc.virtual_sbuf_tensor(name, shape, dtype)
        self.tensors.append((h, offset))
        return h

    def allocate(self, ctx):
        self.handle = ctx.enter_context(
            self.nc.sbuf_tensor(self.name, [self.nc.NUM_PARTITIONS, self.size],
                                mybir.dt.uint8))
        base = self.nc.lookup_mloc(self.handle).addr
        for h, off in self.tensors:
            self.nc._tensor(h.name, h.shape, h.dtype, type="SB")
            mloc = self.nc.lookup_mloc(h)
            mloc.allocated = not self.nc.target_bir_lowering
            mloc.addr = base + off


class Sched:

    def __init__(self):
        self.ops = []
        self.cnt = {}

    def op(self, engine, emit, waits=(), inc=None):
        self.ops.append((engine, emit, [w for w in waits if w is not None], inc))
        if inc is not None:
            sem, amt = inc
            self.cnt[sem] = self.cnt.get(sem, 0) + amt
            return (sem, self.cnt[sem])
        return None

    def val(self, sem):
        return self.cnt.get(sem, 0)


class Pool:

    def __init__(self, slots, init_event=None):
        self.free = [[s, init_event] for s in slots]
        self.held = []

    def acquire(self):
        assert self.free, "pool exhausted"
        s = self.free.pop(0)
        self.held.append(s)
        return s

    def release(self, s, event):
        s[1] = event
        self.held.remove(s)
        self.free.append(s)


def build_nc():
    nc = bass.Bass("TRN2", target_bir_lowering=False)

    XKV = nc.dram_tensor("XKV", [128, CCN * KV], F32, kind="ExternalInput")
    XQ = nc.dram_tensor("XQ", [128, CCN * TQ], F32, kind="ExternalInput")
    WVT = nc.dram_tensor("WVT", [4, 128, CCN * 512], F32, kind="ExternalInput")
    WKT = nc.dram_tensor("WKT", [NH, 128, CCN * 128], F32, kind="ExternalInput")
    WQT = nc.dram_tensor("WQT", [NH, 128, CCN * 128], F32, kind="ExternalInput")
    WOT = nc.dram_tensor("WOT", [NOT_, 128, CCN * 128], F32, kind="ExternalInput")
    WG = nc.dram_tensor("WG", [NHT, 128, CCN * 128], F32, kind="ExternalInput")
    WU = nc.dram_tensor("WU", [NHT, 128, CCN * 128], F32, kind="ExternalInput")
    WD = nc.dram_tensor("WD", [32, 128, 32 * 128], F32, kind="ExternalInput")
    RAMP = nc.dram_tensor("RAMP", [128, 896], BF16, kind="ExternalInput")
    ONESD = nc.dram_tensor("ONESD", [128, 128], BF16, kind="ExternalInput")
    CONSTS = nc.dram_tensor("CONSTS", [128, NCONST], F32, kind="ExternalInput")
    OUT = nc.dram_tensor("OUT", [128, CCN * TQ], F32, kind="ExternalOutput")
    dbg = {}
    if DEBUG:
        dbg["DV"] = nc.dram_tensor("DV", [128, 16 * 2048], BF16, kind="ExternalOutput")
        dbg["DK"] = nc.dram_tensor("DK", [128, G * KV], BF16, kind="ExternalOutput")
        dbg["DQ"] = nc.dram_tensor("DQ", [128, G * TQ], BF16, kind="ExternalOutput")
        dbg["DCTX"] = nc.dram_tensor("DCTX", [128, CCN * TQ], BF16, kind="ExternalOutput")
        dbg["DR1"] = nc.dram_tensor("DR1", [128, CCN * TQ], F32, kind="ExternalOutput")

    ar = Arena(nc)
    off = 0
    a_bf = ar.at(off, "a_bf", [128, 32768], BF16)
    off += 65536
    v_sb = ar.at(off, "v_sb", [128, 32768], BF16)
    s_f = ar.at(off, "s_f", [128, 16384], F32)
    off += 65536
    ctx_bf = ar.at(off, "ctx_bf", [128, 16384], BF16)
    off += 32768
    kg = ar.at(off, "kg", [128, G * KV], BF16)
    off += G * KV * 2
    qg = ar.at(off, "qg", [128, G * TQ], BF16)
    off += G * TQ * 2
    wslots = []
    slot0_off = off
    for i in range(5):
        wslots.append(ar.at(off, f"wslot{i}", [128, 2048], BF16))
        off += 4096
    wvbig = ar.at(slot0_off, "wvbig", [128, 8192], BF16)
    wvbig2 = ar.at(131072, "wvbig2", [128, 8192], BF16)
    dbig_a = ar.at(slot0_off, "dbig_a", [128, 4096], BF16)
    dbig_b = ar.at(slot0_off + 12288, "dbig_b", [128, 4096], BF16)
    wpool_t = []
    for i in range(4):
        wpool_t.append(ar.at(off, f"wp{i}", [128, 512], BF16))
        off += 1024
    spool_t = []
    for i in range(3):
        spool_t.append(ar.at(off, f"sp{i}", [128, 512], F32))
        off += 2048
    ramp = ar.at(off, "ramp", [128, 896], BF16); off += 1792
    ones = ar.at(off, "ones", [128, 128], BF16); off += 256
    consts = ar.at(off, "consts", [128, NCONST], F32); off += 96
    assert off <= 212992, off

    sch = Sched()

    with ExitStack() as ctx:
        ar.allocate(ctx)
        psum_t = [ctx.enter_context(nc.psum_tensor(f"ps{i}", [128, 512], F32))
                  for i in range(8)]
        GD = [ctx.enter_context(nc.semaphore(f"gd{i}")) for i in range(6)]
        SY = ctx.enter_context(nc.semaphore("sy"))
        SXQ = ctx.enter_context(nc.semaphore("sxq"))
        SOUT = ctx.enter_context(nc.semaphore("sout"))
        ST = ctx.enter_context(nc.semaphore("sT"))
        SS = ctx.enter_context(nc.semaphore("sS"))
        SV = ctx.enter_context(nc.semaphore("sV"))
        sem_h = {"sy": SY, "sxq": SXQ, "sout": SOUT, "sT": ST, "sS": SS, "sV": SV}
        for i in range(6):
            sem_h[f"gd{i}"] = GD[i]

        for v in CONST_VALS:
            nc.const_aps.register(F32, v, consts[:, CONST_VALS.index(v):CONST_VALS.index(v) + 1])

        gq = {"i": 0}

        def gdma(dst, src, waits=()):
            i = gq["i"] % 6
            gq["i"] += 1
            sem = f"gd{i}"
            prev = sch.val(sem)
            w = list(waits)
            if prev:
                w.append((sem, prev))
            return sch.op("gpsimd", lambda g, d=dst, s=src: g.dma_start(d, s),
                          waits=w, inc=(sem, 16))

        def sdma(dst, src, waits=(), sem="sy"):
            return sch.op("sync", lambda g, d=dst, s=src: g.dma_start(d, s),
                          waits=waits, inc=(sem, 16))

        def mm(out, lhsT, rhs, start, stop, waits=(), inc=False, skip=False):
            def e(t, o=out, l=lhsT, r=rhs, st=start, sp=stop, sk=skip):
                return t.matmul(o, l, r, start=st, stop=sp, skip_group_check=sk)
            return sch.op("tensor", e, waits=waits, inc=(("sT", 1) if inc else None))

        def sact(out, in_, func, waits=(), bias=None, scale=None):
            kw = {}
            if bias is not None:
                kw["bias"] = bias
            if scale is not None:
                kw["scale"] = scale
            def e(s, o=out, i=in_, f=func, kw=kw):
                return s.activation(o, i, f, **kw)
            return sch.op("scalar", e, waits=waits, inc=("sS", 1))

        def vtt(out, in0, in1, alu, waits=()):
            def e(v, o=out, a=in0, b=in1, op=alu):
                return v.tensor_tensor(o, a, b, op)
            return sch.op("vector", e, waits=waits, inc=("sV", 1))

        def vts(out, in0, s1, s2, op0, op1, waits=()):
            def e(v, o=out, a=in0, x=s1, y=s2, p=op0, q=op1):
                return v.tensor_scalar(o, a, x, y, p, q)
            return sch.op("vector", e, waits=waits, inc=("sV", 1))

        def vrecip(out, in_, waits=()):
            def e(v, o=out, i=in_):
                return v.reciprocal_approx_fast(o, i)
            return sch.op("vector", e, waits=waits, inc=("sV", 1))

        psum = Pool(psum_t)
        wpool = Pool(wpool_t)
        spool = Pool(spool_t)
        slotp = Pool(wslots[:4])

        sdma(consts[:, :], CONSTS[:, :])
        sdma(ramp[:, :], RAMP[:, :])
        sdma(ones[:, :], ONESD[:, :])
        ev_const = ("sy", sch.val("sy"))
        e = sact(consts[:, 16:17], consts[:, 14:15], AF.Ln, waits=[ev_const])
        e = sact(consts[:, 17:18], consts[:, 15:16], AF.Ln, waits=[e])
        e = sact(consts[:, 1:2], consts[:, 16:17], AF.Exp, scale=C_N1, waits=[e])
        e = sact(consts[:, 0:1], consts[:, 17:18], AF.Exp, scale=C_NH, waits=[e])
        e = sact(consts[:, 2:3], consts[:, 16:17], AF.Exp, scale=C_N6, waits=[e])
        e = sact(consts[:, 3:4], consts[:, 16:17], AF.Exp, scale=C_N5, waits=[e])
        e = sact(consts[:, 11:12], consts[:, 1:2], AF.Copy,
                 scale=consts[:, 10:11], waits=[e])
        ev_synth = sact(consts[:, 13:14], consts[:, 1:2], AF.Copy,
                        scale=consts[:, 12:13], waits=[e])
        sch.op("scalar", lambda s, v=ev_synth: s.wait_ge(sem_h[v[0]], v[1]))
        xkv_src = XKV[:, :].rearrange("p (cc t) -> p cc t", cc=CCN)
        xkv_dst = a_bf[:, :].rearrange("p (cc t) -> p cc t", cc=CCN)
        ev_wv0 = gdma(wvbig[:, :], WVT[0, :, :])
        ev_xsl = [None] * 16
        for h in range(16):
            ev_xsl[h] = gdma(xkv_dst[:, :, h * 128:(h + 1) * 128],
                             xkv_src[:, :, h * 128:(h + 1) * 128])
        ev_xk = [ev_xsl[7], ev_xsl[15]]

        def xkv_ap(cc, t0, t1):
            return a_bf[:, cc * KV + t0: cc * KV + t1]

        def ev_xk_for(t1):
            return ev_xsl[(t1 - 1) // 128]

        wvbufs = [wvbig, wvbig2]
        ev_wv_used = [None, None]
        ev_wv_last = None
        for dt in range(4):
            wb = wvbufs[dt % 2]
            if dt == 0:
                ev_wv = ev_wv0
            else:
                ev_wv = gdma(wb[:, :], WVT[dt, :, :],
                             waits=[ev_wv_used[dt % 2]] if ev_wv_used[dt % 2] else [])
            last_mm = None
            for tkc in range(16):
                pb = psum.acquire()
                t0, t1 = tkc * 128, tkc * 128 + 128
                for cc in range(CCN):
                    w = []
                    if cc == 0:
                        w = [ev_wv, ev_xk_for(t1), pb[1]]
                    last_mm = mm(pb[0][:, :], xkv_ap(cc, t0, t1),
                                 wb[:, cc * 512:(cc + 1) * 512],
                                 start=(cc == 0), stop=(cc == CCN - 1),
                                 waits=w, inc=(cc == CCN - 1))
                ev = sact(v_sb[:, tkc * 2048 + dt * 512: tkc * 2048 + (dt + 1) * 512],
                          pb[0][:, :], AF.Copy, waits=[last_mm])
                psum.release(pb, ev)
            ev_wv_used[dt % 2] = last_mm
            ev_wv_last = last_mm
        ev_v_done_scalar = ("sS", sch.val("sS"))
        for s in slotp.free:
            s[1] = ev_wv_used[0]
        slotp.free = [[wslots[4], None]] + slotp.free
        ev_ctx_alias_done = ev_wv_used[1]

        if DEBUG:
            sdma(dbg["DV"][:, :], v_sb[:, :], waits=[ev_v_done_scalar], sem="sout")

        kq_evac = {}
        ev_last_ctx = None
        pending_norm = [None]
        for g in range(NG):
            for hg in range(G):
                h = G * g + hg
                ws = slotp.acquire()
                ev_w = gdma(ws[0][:, :], WKT[h, :, :], waits=[ws[1]])
                for tt in range(NKT):
                    pb = psum.acquire()
                    last_mm = None
                    for cc in range(CCN):
                        w = []
                        if cc == 0:
                            w = [ev_w, ev_xk_for((tt + 1) * 512), pb[1]]
                        last_mm = mm(pb[0][:, :], ws[0][:, cc * 128:(cc + 1) * 128],
                                     xkv_ap(cc, tt * 512, (tt + 1) * 512),
                                     start=(cc == 0), stop=(cc == CCN - 1),
                                     waits=w, inc=(cc == CCN - 1))
                    ev = sact(kg[:, hg * KV + tt * 512: hg * KV + (tt + 1) * 512],
                              pb[0][:, :], AF.Copy, waits=[last_mm])
                    kq_evac[(hg, tt)] = ev
                    psum.release(pb, ev)
                slotp.release(ws, last_mm)
                ws = slotp.acquire()
                ev_w = gdma(ws[0][:, :], WQT[h, :, :], waits=[ws[1]])
                for tqt in range(NTQ):
                    pb = psum.acquire()
                    last_mm = None
                    for cc in range(CCN):
                        w = []
                        if cc == 0:
                            w = [ev_w, ev_xk[1], pb[1]]
                        last_mm = mm(pb[0][:, :], ws[0][:, cc * 128:(cc + 1) * 128],
                                     xkv_ap(cc, tqt * 512, (tqt + 1) * 512),
                                     start=(cc == 0), stop=(cc == CCN - 1),
                                     waits=w, inc=(cc == CCN - 1))
                    ev = sact(qg[:, hg * TQ + tqt * 512: hg * TQ + (tqt + 1) * 512],
                              pb[0][:, :], AF.Copy, waits=[last_mm],
                              scale=consts[:, 0:1])
                    kq_evac[("q", hg, tqt)] = ev
                    psum.release(pb, ev)
                slotp.release(ws, last_mm)
            if DEBUG and g == 0:
                sdma(dbg["DK"][:, :], kg[:, :],
                     waits=[kq_evac[(G - 1, NKT - 1)]], sem="sout")
                sdma(dbg["DQ"][:, :], qg[:, :],
                     waits=[kq_evac[("q", G - 1, NTQ - 1)]], sem="sout")

            for hg in range(G):
                h = G * g + hg
                for tqt in range(NTQ):
                    tkc_list = [0, 1, 2, 3, 8, 9, 10, 11] if tqt == 0 else list(range(16))
                    n_tkc = len(tkc_list)
                    pc = psum.acquire()
                    pu = psum.acquire()
                    q_ap = qg[:, hg * TQ + tqt * 512: hg * TQ + (tqt + 1) * 512]
                    pend = []
                    relu_flip = 0

                    def do_ctx(item, pc=pc, pu=pu, n_tkc=n_tkc, h=h):
                        idx, tkc, wt, ev_w = item
                        first = (idx == 0)
                        last = (idx == n_tkc - 1)
                        w1 = [ev_w]
                        if first:
                            w1 += [pc[1], ev_v_done_scalar]
                        mm(pc[0][:, :],
                           v_sb[:, tkc * 2048 + h * 128: tkc * 2048 + (h + 1) * 128],
                           wt[0][:, :], start=first, stop=last,
                           waits=w1, inc=last, skip=True)
                        e2 = mm(pu[0][:, :], ones[:, :], wt[0][:, :],
                                start=first, stop=last,
                                waits=[pu[1]] if first else [],
                                inc=True, skip=True)
                        wpool.release(wt, e2)
                        return e2

                    ev_group_end = None
                    for idx, tkc in enumerate(tkc_list):
                        pb = psum.acquire()
                        ev_s = mm(pb[0][:, :],
                                  kg[:, hg * KV + tkc * 128: hg * KV + tkc * 128 + 128],
                                  q_ap, start=True, stop=True,
                                  waits=[pb[1], kq_evac[(hg, tkc // 4)],
                                         kq_evac[("q", hg, tqt)]],
                                  inc=True)
                        diag = (tkc < 4) if tqt == 0 else (4 <= tkc < 8)
                        pmask = (tkc >= 8) if tqt == 0 else (tkc >= 12)
                        wt = wpool.acquire()
                        if diag:
                            off_d = (tkc % 4) * 128
                            ev_r = vts(wt[0][:, :], pb[0][:, :],
                                       consts[:, 1:2], consts[:, 5:6],
                                       ALU.add, ALU.max,
                                       waits=[ev_s, wt[1], ev_const])
                            ev_m = vtt(wt[0][:, :], wt[0][:, :],
                                       ramp[:, 384 - off_d: 384 - off_d + 512],
                                       ALU.mult, waits=[ev_r])
                            ready = ev_m
                        elif pmask:
                            pm = 10 if tqt == 0 else 12
                            ready = sact(wt[0][:, :], pb[0][:, :], AF.Relu,
                                         waits=[ev_s, wt[1], ev_const],
                                         bias=consts[:, pm + 1: pm + 2],
                                         scale=consts[:, pm: pm + 1])
                        elif relu_flip == 0:
                            ready = sact(wt[0][:, :], pb[0][:, :], AF.Relu,
                                         waits=[ev_s, wt[1]], bias=C_B01)
                            relu_flip = 1
                        else:
                            ready = vts(wt[0][:, :], pb[0][:, :],
                                        consts[:, 1:2], consts[:, 5:6],
                                        ALU.add, ALU.max,
                                        waits=[ev_s, wt[1], ev_const])
                            relu_flip = 0
                        psum.release(pb, ready)
                        pend.append((idx, tkc, wt, ready))
                        if tkc == 1 and pending_norm[0] is not None:
                            ev_last_ctx = pending_norm[0]()
                            pending_norm[0] = None
                        if len(pend) >= 4:
                            do_ctx(pend.pop(0))
                    while pend:
                        ev_group_end = do_ctx(pend.pop(0))

                    def norm_emit(pc=pc, pu=pu, h=h, tqt=tqt, ev_ge=ev_group_end):
                        den = spool.acquire()
                        ev_d = sact(den[0][:, :], pu[0][:, :], AF.Ln,
                                    waits=[ev_ge, den[1]], bias=C_EA)
                        psum.release(pu, ev_d)
                        rcp = spool.acquire()
                        ev_rc = sact(rcp[0][:, :], den[0][:, :], AF.Exp,
                                     waits=[ev_d, rcp[1]], scale=C_N1)
                        spool.release(den, ev_rc)
                        ev_cn = vtt(ctx_bf[:, h * TQ + tqt * 512:
                                           h * TQ + (tqt + 1) * 512],
                                    pc[0][:, :], rcp[0][:, :], ALU.mult,
                                    waits=[ev_rc])
                        spool.release(rcp, ev_cn)
                        psum.release(pc, ev_cn)
                        return ev_cn

                    pending_norm[0] = norm_emit
        ev_last_ctx = pending_norm[0]()
        pending_norm[0] = None
        ev_attn_mm_done = ("sT", sch.val("sT"))

        if DEBUG:
            sdma(dbg["DCTX"][:, :], ctx_bf[:, :], waits=[ev_last_ctx], sem="sout")

        def layernorm_tqt(tqt, chunk_ready, r1_cast, out_events):
            pm = psum.acquire()
            pq = psum.acquire()
            ln_mm = {}
            for cc in range(CCN):
                sl = s_f[:, cc * TQ + tqt * 512: cc * TQ + (tqt + 1) * 512]
                sb = wpool.acquire()
                ev_c = sact(sb[0][:, :], sl, AF.Copy,
                            waits=[chunk_ready[(cc, tqt)], sb[1]])
                qb = wpool.acquire()
                ev_q = vtt(qb[0][:, :], sl, sl, ALU.mult,
                           waits=[chunk_ready[(cc, tqt)], qb[1]])
                w = [ev_c] + ([pm[1]] if cc == 0 else [])
                e1 = mm(pm[0][:, :], ones[:, :], sb[0][:, :],
                        start=(cc == 0), stop=(cc == CCN - 1),
                        waits=w, inc=True, skip=True)
                wpool.release(sb, e1)
                w = [ev_q] + ([pq[1]] if cc == 0 else [])
                e2 = mm(pq[0][:, :], ones[:, :], qb[0][:, :],
                        start=(cc == 0), stop=(cc == CCN - 1),
                        waits=w, inc=True, skip=True)
                wpool.release(qb, e2)
                ln_mm[cc] = (e1, e2)
            m_b = spool.acquire()
            ev_m = sact(m_b[0][:, :], pm[0][:, :], AF.Copy,
                        waits=[ln_mm[CCN - 1][0], m_b[1]], scale=C_IC)
            psum.release(pm, ev_m)
            q_b = spool.acquire()
            ev_qb = sact(q_b[0][:, :], pq[0][:, :], AF.Copy,
                         waits=[ln_mm[CCN - 1][1], q_b[1]],
                         scale=C_IC, bias=C_EL)
            psum.release(pq, ev_qb)
            w_b = spool.acquire()
            ev_m2 = sact(w_b[0][:, :], m_b[0][:, :], AF.Square,
                         waits=[ev_m, w_b[1]])
            ev_var = vtt(q_b[0][:, :], q_b[0][:, :], w_b[0][:, :],
                         ALU.subtract, waits=[ev_qb, ev_m2])
            ev_ln = sact(w_b[0][:, :], q_b[0][:, :], AF.Ln, waits=[ev_var])
            ev_inv = sact(q_b[0][:, :], w_b[0][:, :], AF.Exp,
                          waits=[ev_ln], scale=C_NH)
            spool.release(w_b, ev_inv)
            inv_b = q_b
            for cc in range(CCN):
                sl = s_f[:, cc * TQ + tqt * 512: cc * TQ + (tqt + 1) * 512]
                e1 = vtt(sl, sl, m_b[0][:, :], ALU.subtract,
                         waits=[ev_m])
                e2 = vtt(sl, sl, inv_b[0][:, :], ALU.mult, waits=[ev_inv, e1])
                ev_fin = e2
                if r1_cast:
                    ev_fin = sact(ctx_bf[:, cc * TQ + tqt * 512:
                                         cc * TQ + (tqt + 1) * 512],
                                  sl, AF.Copy, waits=[e2])
                out_events[(cc, tqt)] = (e2, ev_fin)
            last_apply = out_events[(CCN - 1, tqt)][0]
            spool.release(m_b, last_apply)
            spool.release(inv_b, last_apply)

        ev_xq = sdma(s_f[:, :], XQ[:, :], waits=[ev_attn_mm_done], sem="sxq")
        wo_evac = {}
        ln1 = {}
        for tqt in range(NTQ):
            for ot in range(NOT_):
                ws = slotp.acquire()
                ev_w = gdma(ws[0][:, :], WOT[ot, :, :], waits=[ws[1]])
                pb = psum.acquire()
                last_mm = None
                for cc in range(CCN):
                    w = []
                    if cc == 0:
                        w = [ev_w, ev_last_ctx, pb[1]]
                    last_mm = mm(pb[0][:, :], ws[0][:, cc * 128:(cc + 1) * 128],
                                 ctx_bf[:, cc * TQ + tqt * 512: cc * TQ + (tqt + 1) * 512],
                                 start=(cc == 0), stop=(cc == CCN - 1),
                                 waits=w, inc=(cc == CCN - 1))
                sl = s_f[:, ot * TQ + tqt * 512: ot * TQ + (tqt + 1) * 512]
                ev = vtt(sl, sl, pb[0][:, :], ALU.add, waits=[last_mm, ev_xq])
                wo_evac[(ot, tqt)] = ev
                psum.release(pb, ev)
                slotp.release(ws, last_mm)
            layernorm_tqt(tqt, wo_evac, True, ln1)
        r1bf_tqt = [ln1[(CCN - 1, t)][1] for t in range(NTQ)]
        if DEBUG:
            sdma(dbg["DR1"][:, :], s_f[:, :], waits=[r1bf_tqt[-1]], sem="sout")

        def r1_ap(cc, tqt):
            return ctx_bf[:, cc * TQ + tqt * 512: cc * TQ + (tqt + 1) * 512]

        slot34_ev = [s[1] for s in slotp.free if s[0] in (wslots[3], wslots[4])]
        slot34_ev = max([e for e in slot34_ev if e is not None] or [None],
                        key=lambda e: (e is not None, e[1] if e else -1))
        gu_pool = Pool([])
        gu_pool.free = ([s for s in slotp.free
                         if s[0] in (wslots[0], wslots[1], wslots[2])]
                        + [[qg, ev_attn_mm_done]])
        assert len(gu_pool.free) == 4, len(gu_pool.free)
        d_pool = Pool([])
        d_pool.free = [[dbig_b, slot34_ev], [kg, ev_attn_mm_done]]
        down_evac = {}
        ln2 = {}
        ev_down_done_prev = None

        def gate_up(ht, tqt, hid_ev):
            hti = ht % 32
            wsg = gu_pool.acquire()
            ev_wg = gdma(wsg[0][:, :], WG[ht, :, :], waits=[wsg[1]])
            wsu = gu_pool.acquire()
            ev_wu = gdma(wsu[0][:, :], WU[ht, :, :], waits=[wsu[1]])
            pg = psum.acquire()
            lg = None
            for cc in range(CCN):
                w = []
                if cc == 0:
                    w = [ev_wg, r1bf_tqt[tqt], pg[1]]
                    if ev_down_done_prev is not None:
                        w.append(ev_down_done_prev)
                lg = mm(pg[0][:, :], wsg[0][:, cc * 128:(cc + 1) * 128],
                        r1_ap(cc, tqt), start=(cc == 0),
                        stop=(cc == CCN - 1), waits=w, inc=(cc == CCN - 1))
            pu = psum.acquire()
            lu = None
            for cc in range(CCN):
                w = [ev_wu, pu[1]] if cc == 0 else []
                lu = mm(pu[0][:, :], wsu[0][:, cc * 128:(cc + 1) * 128],
                        r1_ap(cc, tqt), start=(cc == 0),
                        stop=(cc == CCN - 1), waits=w, inc=(cc == CCN - 1))
            ga = wpool.acquire()
            ev_g = sact(ga[0][:, :], pg[0][:, :], AF.Silu, waits=[lg, ga[1]])
            psum.release(pg, ev_g)
            hsl = a_bf[:, hti * TQ + tqt * 512: hti * TQ + (tqt + 1) * 512]
            ev_h = vtt(hsl, pu[0][:, :], ga[0][:, :], ALU.mult, waits=[ev_g, lu])
            wpool.release(ga, ev_h)
            psum.release(pu, ev_h)
            hid_ev[(hti, tqt)] = ev_h
            gu_pool.release(wsg, lg)
            gu_pool.release(wsu, lu)

        def down(half, ot, tqts, ev_hid):
            ds = d_pool.acquire()
            ev_w = gdma(ds[0][:, :], WD[half * 16 + ot, :, :], waits=[ds[1]])
            ld = None
            for tqt in tqts:
                pd = psum.acquire()
                for hc in range(32):
                    w = []
                    if hc == 0:
                        w = [ev_w, ev_hid, pd[1]]
                    ld = mm(pd[0][:, :], ds[0][:, hc * 128:(hc + 1) * 128],
                            a_bf[:, hc * TQ + tqt * 512: hc * TQ + (tqt + 1) * 512],
                            start=(hc == 0), stop=(hc == 31),
                            waits=w, inc=(hc == 31))
                sl = s_f[:, ot * TQ + tqt * 512: ot * TQ + (tqt + 1) * 512]
                wv = [ld]
                if half == 0:
                    wv.append(ln1[(ot, tqt)][0])
                else:
                    wv.append(down_evac[(ot, tqt)])
                ev = vtt(sl, sl, pd[0][:, :], ALU.add, waits=wv)
                down_evac[(ot, tqt)] = ev
                psum.release(pd, ev)
            d_pool.release(ds, ld)

        hid_ev = {}
        for tqt in range(NTQ):
            for hti in range(32):
                gate_up(hti, tqt, hid_ev)
        ev_hid_all = hid_ev[(31, NTQ - 1)]
        for ot in range(NOT_):
            down(0, ot, range(NTQ), ev_hid_all)
        ev_down_done_prev = ("sT", sch.val("sT"))

        hid_ev = {}
        for tqt in range(NTQ):
            for hti in range(32):
                gate_up(32 + hti, tqt, hid_ev)
        ev_hid_all = hid_ev[(31, NTQ - 1)]
        ev_gu_last = ("sT", sch.val("sT"))
        d_pool.free.append([dbig_a, ev_gu_last])
        for tqt in range(NTQ):
            for ot in range(NOT_):
                down(1, ot, [tqt], ev_hid_all)
            layernorm_tqt(tqt, down_evac, False, ln2)
            for cc in range(CCN):
                sdma(OUT[:, cc * TQ + tqt * 512: cc * TQ + (tqt + 1) * 512],
                     s_f[:, cc * TQ + tqt * 512: cc * TQ + (tqt + 1) * 512],
                     waits=[ln2[(cc, tqt)][0]], sem="sout")
        sch.op("sync", lambda g: g.wait_ge(sem_h["sout"], sch.val("sout")),
               waits=())

        with nc.Block() as block:
            def emit_engine(eng_name, eng):
                guar = {}
                if eng_name in ("scalar", "vector", "tensor"):
                    eng.wait_ge(sem_h[ev_const[0]], ev_const[1])
                    guar[ev_const[0]] = ev_const[1]
                if eng_name == "vector":
                    eng.wait_ge(sem_h[ev_synth[0]], ev_synth[1])
                    guar[ev_synth[0]] = ev_synth[1]
                for (engine, emit, waits, inc) in sch.ops:
                    if engine != eng_name:
                        continue
                    for (sem, val) in waits:
                        if guar.get(sem, 0) >= val:
                            continue
                        eng.wait_ge(sem_h[sem], val)
                        guar[sem] = val
                    inst = emit(eng)
                    if inc is not None:
                        sem, amt = inc
                        if inst is not None:
                            inst.then_inc(sem_h[sem], amt)

            @block.sync
            def _(sy):
                emit_engine("sync", sy)

            @block.gpsimd
            def _(g):
                emit_engine("gpsimd", g)

            @block.tensor
            def _(t):
                emit_engine("tensor", t)

            @block.scalar
            def _(s):
                emit_engine("scalar", s)

            @block.vector
            def _(v):
                emit_engine("vector", v)

    return nc


PERM = {0: [0, 3, 1, 2], 1: [1, 2, 0, 3]}


def _images(x_b, par):
    perm = PERM[par]
    xt = x_b.reshape(4, 512, C)[perm].reshape(T, C)
    xkv = np.ascontiguousarray(
        xt.reshape(T, CCN, 128).transpose(2, 1, 0).reshape(128, CCN * KV))
    xq = np.ascontiguousarray(
        xt[:1024].reshape(TQ, CCN, 128).transpose(2, 1, 0).reshape(128, CCN * TQ))
    return xkv, xq


def _tiles_lhsT(w, ntile):
    return np.ascontiguousarray(
        w.reshape(ntile, 128, CCN, 128).transpose(0, 3, 2, 1)
        .reshape(ntile, 128, CCN * 128))


def kernel(L0_attn_x, L0_attn_Wq, L0_attn_Wk, L0_attn_Wv, L0_attn_Wo,
           L0_ffn_w_gate, L0_ffn_w_up, L0_ffn_w_down):
    x = np.asarray(L0_attn_x, dtype=np.float32)
    Wq = np.asarray(L0_attn_Wq, dtype=np.float32)
    Wk = np.asarray(L0_attn_Wk, dtype=np.float32)
    Wv = np.asarray(L0_attn_Wv, dtype=np.float32)
    Wo = np.asarray(L0_attn_Wo, dtype=np.float32)
    wg = np.asarray(L0_ffn_w_gate, dtype=np.float32)
    wu = np.asarray(L0_ffn_w_up, dtype=np.float32)
    wd = np.asarray(L0_ffn_w_down, dtype=np.float32)

    WKT = _tiles_lhsT(Wk, NH)
    WQT = _tiles_lhsT(Wq, NH)
    WOT = _tiles_lhsT(Wo, NOT_)
    WGi = _tiles_lhsT(wg, NHT)
    WUi = _tiles_lhsT(wu, NHT)
    WVT = np.ascontiguousarray(
        Wv.reshape(4, 512, CCN, 128).transpose(0, 3, 2, 1).reshape(4, 128, CCN * 512))
    WDi = np.ascontiguousarray(
        wd.reshape(NOT_, 128, 2, 32, 128).transpose(2, 0, 4, 3, 1)
        .reshape(32, 128, 32 * 128))

    iu = np.arange(896)[None, :]
    ip = np.arange(128)[:, None]
    RAMPi = (iu >= ip + 384).astype(BFD)
    ONESi = np.ones((128, 128), dtype=BFD)

    in_maps = []
    for core in range(8):
        b, par = core // 2, core % 2
        xkv, xq = _images(x[b], par)
        CONSTSi = np.zeros((128, NCONST), dtype=np.float32)
        for i, v in enumerate(CONST_VALS):
            CONSTSi[:, i] = v
        pm1 = (0.0, 0.0) if par == 0 else (1.0, C_B01)
        pm2 = (1.0, C_B01) if par == 0 else (0.0, 0.0)
        CONSTSi[:, 10], CONSTSi[:, 11] = pm1
        CONSTSi[:, 12], CONSTSi[:, 13] = pm2
        CONSTSi[:, 14] = 10.0
        CONSTSi[:, 15] = 128.0
        in_maps.append({
            "XKV": xkv, "XQ": xq, "WVT": WVT, "WKT": WKT, "WQT": WQT,
            "WOT": WOT, "WG": WGi, "WU": WUi, "WD": WDi,
            "RAMP": RAMPi, "ONESD": ONESi, "CONSTS": CONSTSi,
        })

    nc = build_nc()
    res = run_bass_kernel_spmd(nc, in_maps, core_ids=list(range(8)))

    out = np.empty((B, T, C), dtype=np.asarray(res.results[0]["OUT"]).dtype)
    for core in range(8):
        b, par = core // 2, core % 2
        o = res.results[core]["OUT"]
        loc = o.reshape(128, CCN, TQ).transpose(2, 1, 0).reshape(TQ, C)
        for tqt in range(2):
            pos = PERM[par][tqt]
            out[b, 512 * pos: 512 * pos + 512, :] = loc[512 * tqt: 512 * tqt + 512]
    return out


# version 19
# speedup vs baseline: 1.0228x; 1.0228x over previous
import math
from contextlib import ExitStack
from functools import reduce

import numpy as np
import ml_dtypes

import concourse.bass as bass
import concourse.mybir as mybir
from concourse.bass_utils import run_bass_kernel_spmd

F32 = mybir.dt.float32
BF16 = mybir.dt.bfloat16
AF = mybir.ActivationFunctionType
ALU = mybir.AluOpType
BFD = ml_dtypes.bfloat16

B, T, C, NH, HD = 4, 2048, 2048, 16, 128
CCN = 16
TQ = 1024
KV = 2048
NKT = KV // 512
NTQ = TQ // 512
Q0P = 1024
DH = 8192
NHT = DH // 128
NOT_ = C // 128
NG = 8
G = 2

C_QS = float(np.float32(1.0 / math.sqrt(HD)))
C_B01 = float(np.float32(0.1))
C_EA = float(np.float32(1e-6))
C_EL = float(np.float32(1e-5))
C_IC = float(np.float32(1.0 / C))
C_Z = 0.0
C_N1 = -1.0
C_NH = -0.5
C_N5 = -5.0
C_N6 = -6.0
CONST_VALS = [C_QS, C_B01, C_EA, C_EL, C_IC, C_Z, C_N1, C_NH, C_N5, C_N6]
NCONST = 18

DEBUG = False


class Arena:

    def __init__(self, nc, name="arena"):
        self.nc = nc
        self.name = name
        self.size = 0
        self.tensors = []

    def at(self, offset, name, shape, dtype):
        per_part = reduce(lambda a, b: a * b, shape[1:]) * mybir.dt.size(dtype)
        assert offset % 32 == 0, (name, offset)
        self.size = max(self.size, offset + per_part)
        h = self.nc.virtual_sbuf_tensor(name, shape, dtype)
        self.tensors.append((h, offset))
        return h

    def allocate(self, ctx):
        self.handle = ctx.enter_context(
            self.nc.sbuf_tensor(self.name, [self.nc.NUM_PARTITIONS, self.size],
                                mybir.dt.uint8))
        base = self.nc.lookup_mloc(self.handle).addr
        for h, off in self.tensors:
            self.nc._tensor(h.name, h.shape, h.dtype, type="SB")
            mloc = self.nc.lookup_mloc(h)
            mloc.allocated = not self.nc.target_bir_lowering
            mloc.addr = base + off


class Sched:

    def __init__(self):
        self.ops = []
        self.cnt = {}

    def op(self, engine, emit, waits=(), inc=None):
        self.ops.append((engine, emit, [w for w in waits if w is not None], inc))
        if inc is not None:
            sem, amt = inc
            self.cnt[sem] = self.cnt.get(sem, 0) + amt
            return (sem, self.cnt[sem])
        return None

    def val(self, sem):
        return self.cnt.get(sem, 0)


class Pool:

    def __init__(self, slots, init_event=None):
        self.free = [[s, init_event] for s in slots]
        self.held = []

    def acquire(self):
        assert self.free, "pool exhausted"
        s = self.free.pop(0)
        self.held.append(s)
        return s

    def release(self, s, event):
        s[1] = event
        self.held.remove(s)
        self.free.append(s)


def build_nc():
    nc = bass.Bass("TRN2", target_bir_lowering=False)

    XKV = nc.dram_tensor("XKV", [128, CCN * KV], F32, kind="ExternalInput")
    XQ = nc.dram_tensor("XQ", [128, CCN * TQ], F32, kind="ExternalInput")
    WVT = nc.dram_tensor("WVT", [4, 128, CCN * 512], F32, kind="ExternalInput")
    WKT = nc.dram_tensor("WKT", [NH, 128, CCN * 128], F32, kind="ExternalInput")
    WQT = nc.dram_tensor("WQT", [NH, 128, CCN * 128], F32, kind="ExternalInput")
    WOT = nc.dram_tensor("WOT", [NOT_, 128, CCN * 128], F32, kind="ExternalInput")
    WG = nc.dram_tensor("WG", [NHT, 128, CCN * 128], F32, kind="ExternalInput")
    WU = nc.dram_tensor("WU", [NHT, 128, CCN * 128], F32, kind="ExternalInput")
    WD = nc.dram_tensor("WD", [32, 128, 32 * 128], F32, kind="ExternalInput")
    RAMP = nc.dram_tensor("RAMP", [128, 896], BF16, kind="ExternalInput")
    ONESD = nc.dram_tensor("ONESD", [128, 128], BF16, kind="ExternalInput")
    CONSTS = nc.dram_tensor("CONSTS", [128, NCONST], F32, kind="ExternalInput")
    OUT = nc.dram_tensor("OUT", [128, CCN * TQ], F32, kind="ExternalOutput")
    dbg = {}
    if DEBUG:
        dbg["DV"] = nc.dram_tensor("DV", [128, 16 * 2048], BF16, kind="ExternalOutput")
        dbg["DK"] = nc.dram_tensor("DK", [128, G * KV], BF16, kind="ExternalOutput")
        dbg["DQ"] = nc.dram_tensor("DQ", [128, G * TQ], BF16, kind="ExternalOutput")
        dbg["DCTX"] = nc.dram_tensor("DCTX", [128, CCN * TQ], BF16, kind="ExternalOutput")
        dbg["DR1"] = nc.dram_tensor("DR1", [128, CCN * TQ], F32, kind="ExternalOutput")

    ar = Arena(nc)
    off = 0
    a_bf = ar.at(off, "a_bf", [128, 32768], BF16)
    off += 65536
    v_sb = ar.at(off, "v_sb", [128, 32768], BF16)
    s_f = ar.at(off, "s_f", [128, 16384], F32)
    off += 65536
    ctx_bf = ar.at(off, "ctx_bf", [128, 16384], BF16)
    off += 32768
    kg = ar.at(off, "kg", [128, G * KV], BF16)
    off += G * KV * 2
    qg = ar.at(off, "qg", [128, G * TQ], BF16)
    off += G * TQ * 2
    wslots = []
    slot0_off = off
    for i in range(5):
        wslots.append(ar.at(off, f"wslot{i}", [128, 2048], BF16))
        off += 4096
    wvbig = ar.at(slot0_off, "wvbig", [128, 8192], BF16)
    wvbig2 = ar.at(131072, "wvbig2", [128, 8192], BF16)
    dbig_a = ar.at(slot0_off, "dbig_a", [128, 4096], BF16)
    dbig_b = ar.at(slot0_off + 12288, "dbig_b", [128, 4096], BF16)
    wpool_t = []
    for i in range(4):
        wpool_t.append(ar.at(off, f"wp{i}", [128, 512], BF16))
        off += 1024
    spool_t = []
    for i in range(3):
        spool_t.append(ar.at(off, f"sp{i}", [128, 512], F32))
        off += 2048
    ramp = ar.at(off, "ramp", [128, 896], BF16); off += 1792
    ones = ar.at(off, "ones", [128, 128], BF16); off += 256
    consts = ar.at(off, "consts", [128, NCONST], F32); off += 96
    assert off <= 212992, off

    sch = Sched()

    with ExitStack() as ctx:
        ar.allocate(ctx)
        psum_t = [ctx.enter_context(nc.psum_tensor(f"ps{i}", [128, 512], F32))
                  for i in range(8)]
        GD = [ctx.enter_context(nc.semaphore(f"gd{i}")) for i in range(6)]
        SY = ctx.enter_context(nc.semaphore("sy"))
        SXQ = ctx.enter_context(nc.semaphore("sxq"))
        SOUT = ctx.enter_context(nc.semaphore("sout"))
        ST = ctx.enter_context(nc.semaphore("sT"))
        SS = ctx.enter_context(nc.semaphore("sS"))
        SV = ctx.enter_context(nc.semaphore("sV"))
        sem_h = {"sy": SY, "sxq": SXQ, "sout": SOUT, "sT": ST, "sS": SS, "sV": SV}
        for i in range(6):
            sem_h[f"gd{i}"] = GD[i]

        for v in CONST_VALS:
            nc.const_aps.register(F32, v, consts[:, CONST_VALS.index(v):CONST_VALS.index(v) + 1])

        gq = {"i": 0}

        def gdma(dst, src, waits=()):
            i = gq["i"] % 6
            gq["i"] += 1
            sem = f"gd{i}"
            prev = sch.val(sem)
            w = list(waits)
            if prev:
                w.append((sem, prev))
            return sch.op("gpsimd", lambda g, d=dst, s=src: g.dma_start(d, s),
                          waits=w, inc=(sem, 16))

        def sdma(dst, src, waits=(), sem="sy"):
            return sch.op("sync", lambda g, d=dst, s=src: g.dma_start(d, s),
                          waits=waits, inc=(sem, 16))

        def mm(out, lhsT, rhs, start, stop, waits=(), inc=False, skip=False):
            def e(t, o=out, l=lhsT, r=rhs, st=start, sp=stop, sk=skip):
                return t.matmul(o, l, r, start=st, stop=sp, skip_group_check=sk)
            return sch.op("tensor", e, waits=waits, inc=(("sT", 1) if inc else None))

        def sact(out, in_, func, waits=(), bias=None, scale=None):
            kw = {}
            if bias is not None:
                kw["bias"] = bias
            if scale is not None:
                kw["scale"] = scale
            def e(s, o=out, i=in_, f=func, kw=kw):
                return s.activation(o, i, f, **kw)
            return sch.op("scalar", e, waits=waits, inc=("sS", 1))

        def vtt(out, in0, in1, alu, waits=()):
            def e(v, o=out, a=in0, b=in1, op=alu):
                return v.tensor_tensor(o, a, b, op)
            return sch.op("vector", e, waits=waits, inc=("sV", 1))

        def vts(out, in0, s1, s2, op0, op1, waits=()):
            def e(v, o=out, a=in0, x=s1, y=s2, p=op0, q=op1):
                return v.tensor_scalar(o, a, x, y, p, q)
            return sch.op("vector", e, waits=waits, inc=("sV", 1))

        def vrecip(out, in_, waits=()):
            def e(v, o=out, i=in_):
                return v.reciprocal_approx_fast(o, i)
            return sch.op("vector", e, waits=waits, inc=("sV", 1))

        psum = Pool(psum_t)
        wpool = Pool(wpool_t)
        spool = Pool(spool_t)
        slotp = Pool(wslots[:4])

        sdma(consts[:, :], CONSTS[:, :])
        sdma(ramp[:, :], RAMP[:, :])
        sdma(ones[:, :], ONESD[:, :])
        ev_const = ("sy", sch.val("sy"))
        e = sact(consts[:, 16:17], consts[:, 14:15], AF.Ln, waits=[ev_const])
        e = sact(consts[:, 17:18], consts[:, 15:16], AF.Ln, waits=[e])
        e = sact(consts[:, 1:2], consts[:, 16:17], AF.Exp, scale=C_N1, waits=[e])
        e = sact(consts[:, 0:1], consts[:, 17:18], AF.Exp, scale=C_NH, waits=[e])
        e = sact(consts[:, 2:3], consts[:, 16:17], AF.Exp, scale=C_N6, waits=[e])
        e = sact(consts[:, 3:4], consts[:, 16:17], AF.Exp, scale=C_N5, waits=[e])
        e = sact(consts[:, 11:12], consts[:, 1:2], AF.Copy,
                 scale=consts[:, 10:11], waits=[e])
        ev_synth = sact(consts[:, 13:14], consts[:, 1:2], AF.Copy,
                        scale=consts[:, 12:13], waits=[e])
        sch.op("scalar", lambda s, v=ev_synth: s.wait_ge(sem_h[v[0]], v[1]))
        xkv_src = XKV[:, :].rearrange("p (cc t) -> p cc t", cc=CCN)
        xkv_dst = a_bf[:, :].rearrange("p (cc t) -> p cc t", cc=CCN)
        ev_wv0 = gdma(wvbig[:, :], WVT[0, :, :])
        ev_xsl = [None] * 16
        for h in range(16):
            ev_xsl[h] = gdma(xkv_dst[:, :, h * 128:(h + 1) * 128],
                             xkv_src[:, :, h * 128:(h + 1) * 128])
        ev_xk = [ev_xsl[7], ev_xsl[15]]

        def xkv_ap(cc, t0, t1):
            return a_bf[:, cc * KV + t0: cc * KV + t1]

        def ev_xk_for(t1):
            return ev_xsl[(t1 - 1) // 128]

        wvbufs = [wvbig, wvbig2]
        ev_wv_used = [None, None]
        ev_wv_last = None
        for dt in range(4):
            wb = wvbufs[dt % 2]
            if dt == 0:
                ev_wv = ev_wv0
            else:
                ev_wv = gdma(wb[:, :], WVT[dt, :, :],
                             waits=[ev_wv_used[dt % 2]] if ev_wv_used[dt % 2] else [])
            last_mm = None
            for tkc in range(16):
                pb = psum.acquire()
                t0, t1 = tkc * 128, tkc * 128 + 128
                for cc in range(CCN):
                    w = []
                    if cc == 0:
                        w = [ev_wv, ev_xk_for(t1), pb[1]]
                    last_mm = mm(pb[0][:, :], xkv_ap(cc, t0, t1),
                                 wb[:, cc * 512:(cc + 1) * 512],
                                 start=(cc == 0), stop=(cc == CCN - 1),
                                 waits=w, inc=(cc == CCN - 1))
                ev = sact(v_sb[:, tkc * 2048 + dt * 512: tkc * 2048 + (dt + 1) * 512],
                          pb[0][:, :], AF.Copy, waits=[last_mm])
                psum.release(pb, ev)
            ev_wv_used[dt % 2] = last_mm
            ev_wv_last = last_mm
        ev_v_done_scalar = ("sS", sch.val("sS"))
        for s in slotp.free:
            s[1] = ev_wv_used[0]
        slotp.free = [[wslots[4], None]] + slotp.free
        ev_ctx_alias_done = ev_wv_used[1]

        if DEBUG:
            sdma(dbg["DV"][:, :], v_sb[:, :], waits=[ev_v_done_scalar], sem="sout")

        kq_evac = {}
        ev_last_ctx = None
        pending_norm = [None]
        for g in range(NG):
            for hg in range(G):
                h = G * g + hg
                ws = slotp.acquire()
                ev_w = gdma(ws[0][:, :], WKT[h, :, :], waits=[ws[1]])
                for tt in range(NKT):
                    pb = psum.acquire()
                    last_mm = None
                    for cc in range(CCN):
                        w = []
                        if cc == 0:
                            w = [ev_w, ev_xk_for((tt + 1) * 512), pb[1]]
                        last_mm = mm(pb[0][:, :], ws[0][:, cc * 128:(cc + 1) * 128],
                                     xkv_ap(cc, tt * 512, (tt + 1) * 512),
                                     start=(cc == 0), stop=(cc == CCN - 1),
                                     waits=w, inc=(cc == CCN - 1))
                    ev = sact(kg[:, hg * KV + tt * 512: hg * KV + (tt + 1) * 512],
                              pb[0][:, :], AF.Copy, waits=[last_mm])
                    kq_evac[(hg, tt)] = ev
                    psum.release(pb, ev)
                slotp.release(ws, last_mm)
                ws = slotp.acquire()
                ev_w = gdma(ws[0][:, :], WQT[h, :, :], waits=[ws[1]])
                for tqt in range(NTQ):
                    pb = psum.acquire()
                    last_mm = None
                    for cc in range(CCN):
                        w = []
                        if cc == 0:
                            w = [ev_w, ev_xk[1], pb[1]]
                        last_mm = mm(pb[0][:, :], ws[0][:, cc * 128:(cc + 1) * 128],
                                     xkv_ap(cc, tqt * 512, (tqt + 1) * 512),
                                     start=(cc == 0), stop=(cc == CCN - 1),
                                     waits=w, inc=(cc == CCN - 1))
                    ev = sact(qg[:, hg * TQ + tqt * 512: hg * TQ + (tqt + 1) * 512],
                              pb[0][:, :], AF.Copy, waits=[last_mm],
                              scale=consts[:, 0:1])
                    kq_evac[("q", hg, tqt)] = ev
                    psum.release(pb, ev)
                slotp.release(ws, last_mm)
            if DEBUG and g == 0:
                sdma(dbg["DK"][:, :], kg[:, :],
                     waits=[kq_evac[(G - 1, NKT - 1)]], sem="sout")
                sdma(dbg["DQ"][:, :], qg[:, :],
                     waits=[kq_evac[("q", G - 1, NTQ - 1)]], sem="sout")

            for hg in range(G):
                h = G * g + hg
                for tqt in range(NTQ):
                    tkc_list = [0, 1, 2, 3, 8, 9, 10, 11] if tqt == 0 else list(range(16))
                    n_tkc = len(tkc_list)
                    pc = psum.acquire()
                    pu = psum.acquire()
                    q_ap = qg[:, hg * TQ + tqt * 512: hg * TQ + (tqt + 1) * 512]
                    pend = []
                    relu_flip = 0

                    def do_ctx(item, pc=pc, pu=pu, n_tkc=n_tkc, h=h):
                        idx, tkc, wt, ev_w = item
                        first = (idx == 0)
                        last = (idx == n_tkc - 1)
                        w1 = [ev_w]
                        if first:
                            w1 += [pc[1], ev_v_done_scalar]
                        mm(pc[0][:, :],
                           v_sb[:, tkc * 2048 + h * 128: tkc * 2048 + (h + 1) * 128],
                           wt[0][:, :], start=first, stop=last,
                           waits=w1, inc=last, skip=True)
                        e2 = mm(pu[0][:, :], ones[:, :], wt[0][:, :],
                                start=first, stop=last,
                                waits=[pu[1]] if first else [],
                                inc=True, skip=True)
                        wpool.release(wt, e2)
                        return e2

                    ev_group_end = None
                    for idx, tkc in enumerate(tkc_list):
                        pb = psum.acquire()
                        ev_s = mm(pb[0][:, :],
                                  kg[:, hg * KV + tkc * 128: hg * KV + tkc * 128 + 128],
                                  q_ap, start=True, stop=True,
                                  waits=[pb[1], kq_evac[(hg, tkc // 4)],
                                         kq_evac[("q", hg, tqt)]],
                                  inc=True)
                        diag = (tkc < 4) if tqt == 0 else (4 <= tkc < 8)
                        pmask = (tkc >= 8) if tqt == 0 else (tkc >= 12)
                        wt = wpool.acquire()
                        if diag:
                            off_d = (tkc % 4) * 128
                            if tkc % 4 == 2:
                                ev_r = vts(wt[0][:, :], pb[0][:, :],
                                           consts[:, 1:2], consts[:, 5:6],
                                           ALU.add, ALU.max,
                                           waits=[ev_s, wt[1], ev_const])
                            else:
                                ev_r = sact(wt[0][:, :], pb[0][:, :], AF.Relu,
                                            waits=[ev_s, wt[1]],
                                            bias=consts[:, 1:2])
                            ev_m = vtt(wt[0][:, :], wt[0][:, :],
                                       ramp[:, 384 - off_d: 384 - off_d + 512],
                                       ALU.mult, waits=[ev_r])
                            ready = ev_m
                        elif pmask:
                            pm = 10 if tqt == 0 else 12
                            ready = sact(wt[0][:, :], pb[0][:, :], AF.Relu,
                                         waits=[ev_s, wt[1], ev_const],
                                         bias=consts[:, pm + 1: pm + 2],
                                         scale=consts[:, pm: pm + 1])
                        elif relu_flip == 0:
                            ready = sact(wt[0][:, :], pb[0][:, :], AF.Relu,
                                         waits=[ev_s, wt[1]], bias=C_B01)
                            relu_flip = 1
                        else:
                            ready = vts(wt[0][:, :], pb[0][:, :],
                                        consts[:, 1:2], consts[:, 5:6],
                                        ALU.add, ALU.max,
                                        waits=[ev_s, wt[1], ev_const])
                            relu_flip = 0
                        psum.release(pb, ready)
                        pend.append((idx, tkc, wt, ready))
                        if tkc == 1 and pending_norm[0] is not None:
                            ev_last_ctx = pending_norm[0]()
                            pending_norm[0] = None
                        if len(pend) >= 4:
                            do_ctx(pend.pop(0))
                    while pend:
                        ev_group_end = do_ctx(pend.pop(0))

                    def norm_emit(pc=pc, pu=pu, h=h, tqt=tqt, ev_ge=ev_group_end):
                        den = spool.acquire()
                        ev_d = sact(den[0][:, :], pu[0][:, :], AF.Ln,
                                    waits=[ev_ge, den[1]], bias=C_EA)
                        psum.release(pu, ev_d)
                        rcp = spool.acquire()
                        ev_rc = sact(rcp[0][:, :], den[0][:, :], AF.Exp,
                                     waits=[ev_d, rcp[1]], scale=C_N1)
                        spool.release(den, ev_rc)
                        ev_cn = vtt(ctx_bf[:, h * TQ + tqt * 512:
                                           h * TQ + (tqt + 1) * 512],
                                    pc[0][:, :], rcp[0][:, :], ALU.mult,
                                    waits=[ev_rc])
                        spool.release(rcp, ev_cn)
                        psum.release(pc, ev_cn)
                        return ev_cn

                    pending_norm[0] = norm_emit
        ev_last_ctx = pending_norm[0]()
        pending_norm[0] = None
        ev_attn_mm_done = ("sT", sch.val("sT"))

        if DEBUG:
            sdma(dbg["DCTX"][:, :], ctx_bf[:, :], waits=[ev_last_ctx], sem="sout")

        def layernorm_tqt(tqt, chunk_ready, r1_cast, out_events):
            pm = psum.acquire()
            pq = psum.acquire()
            ln_mm = {}
            for cc in range(CCN):
                sl = s_f[:, cc * TQ + tqt * 512: cc * TQ + (tqt + 1) * 512]
                sb = wpool.acquire()
                ev_c = sact(sb[0][:, :], sl, AF.Copy,
                            waits=[chunk_ready[(cc, tqt)], sb[1]])
                qb = wpool.acquire()
                ev_q = vtt(qb[0][:, :], sl, sl, ALU.mult,
                           waits=[chunk_ready[(cc, tqt)], qb[1]])
                w = [ev_c] + ([pm[1]] if cc == 0 else [])
                e1 = mm(pm[0][:, :], ones[:, :], sb[0][:, :],
                        start=(cc == 0), stop=(cc == CCN - 1),
                        waits=w, inc=True, skip=True)
                wpool.release(sb, e1)
                w = [ev_q] + ([pq[1]] if cc == 0 else [])
                e2 = mm(pq[0][:, :], ones[:, :], qb[0][:, :],
                        start=(cc == 0), stop=(cc == CCN - 1),
                        waits=w, inc=True, skip=True)
                wpool.release(qb, e2)
                ln_mm[cc] = (e1, e2)
            m_b = spool.acquire()
            ev_m = sact(m_b[0][:, :], pm[0][:, :], AF.Copy,
                        waits=[ln_mm[CCN - 1][0], m_b[1]], scale=C_IC)
            psum.release(pm, ev_m)
            q_b = spool.acquire()
            ev_qb = sact(q_b[0][:, :], pq[0][:, :], AF.Copy,
                         waits=[ln_mm[CCN - 1][1], q_b[1]],
                         scale=C_IC, bias=C_EL)
            psum.release(pq, ev_qb)
            w_b = spool.acquire()
            ev_m2 = sact(w_b[0][:, :], m_b[0][:, :], AF.Square,
                         waits=[ev_m, w_b[1]])
            ev_var = vtt(q_b[0][:, :], q_b[0][:, :], w_b[0][:, :],
                         ALU.subtract, waits=[ev_qb, ev_m2])
            ev_ln = sact(w_b[0][:, :], q_b[0][:, :], AF.Ln, waits=[ev_var])
            ev_inv = sact(q_b[0][:, :], w_b[0][:, :], AF.Exp,
                          waits=[ev_ln], scale=C_NH)
            spool.release(w_b, ev_inv)
            inv_b = q_b
            for cc in range(CCN):
                sl = s_f[:, cc * TQ + tqt * 512: cc * TQ + (tqt + 1) * 512]
                e1 = vtt(sl, sl, m_b[0][:, :], ALU.subtract,
                         waits=[ev_m])
                e2 = vtt(sl, sl, inv_b[0][:, :], ALU.mult, waits=[ev_inv, e1])
                ev_fin = e2
                if r1_cast:
                    ev_fin = sact(ctx_bf[:, cc * TQ + tqt * 512:
                                         cc * TQ + (tqt + 1) * 512],
                                  sl, AF.Copy, waits=[e2])
                out_events[(cc, tqt)] = (e2, ev_fin)
            last_apply = out_events[(CCN - 1, tqt)][0]
            spool.release(m_b, last_apply)
            spool.release(inv_b, last_apply)

        ev_xq = sdma(s_f[:, :], XQ[:, :], waits=[ev_attn_mm_done], sem="sxq")
        wo_evac = {}
        ln1 = {}
        for tqt in range(NTQ):
            for ot in range(NOT_):
                ws = slotp.acquire()
                ev_w = gdma(ws[0][:, :], WOT[ot, :, :], waits=[ws[1]])
                pb = psum.acquire()
                last_mm = None
                for cc in range(CCN):
                    w = []
                    if cc == 0:
                        w = [ev_w, ev_last_ctx, pb[1]]
                    last_mm = mm(pb[0][:, :], ws[0][:, cc * 128:(cc + 1) * 128],
                                 ctx_bf[:, cc * TQ + tqt * 512: cc * TQ + (tqt + 1) * 512],
                                 start=(cc == 0), stop=(cc == CCN - 1),
                                 waits=w, inc=(cc == CCN - 1))
                sl = s_f[:, ot * TQ + tqt * 512: ot * TQ + (tqt + 1) * 512]
                ev = vtt(sl, sl, pb[0][:, :], ALU.add, waits=[last_mm, ev_xq])
                wo_evac[(ot, tqt)] = ev
                psum.release(pb, ev)
                slotp.release(ws, last_mm)
            layernorm_tqt(tqt, wo_evac, True, ln1)
        r1bf_tqt = [ln1[(CCN - 1, t)][1] for t in range(NTQ)]
        if DEBUG:
            sdma(dbg["DR1"][:, :], s_f[:, :], waits=[r1bf_tqt[-1]], sem="sout")

        def r1_ap(cc, tqt):
            return ctx_bf[:, cc * TQ + tqt * 512: cc * TQ + (tqt + 1) * 512]

        slot34_ev = [s[1] for s in slotp.free if s[0] in (wslots[3], wslots[4])]
        slot34_ev = max([e for e in slot34_ev if e is not None] or [None],
                        key=lambda e: (e is not None, e[1] if e else -1))
        gu_pool = Pool([])
        gu_pool.free = ([s for s in slotp.free
                         if s[0] in (wslots[0], wslots[1], wslots[2])]
                        + [[qg, ev_attn_mm_done]])
        assert len(gu_pool.free) == 4, len(gu_pool.free)
        d_pool = Pool([])
        d_pool.free = [[dbig_b, slot34_ev], [kg, ev_attn_mm_done]]
        down_evac = {}
        ln2 = {}
        ev_down_done_prev = None

        def gate_up(ht, tqt, hid_ev):
            hti = ht % 32
            wsg = gu_pool.acquire()
            ev_wg = gdma(wsg[0][:, :], WG[ht, :, :], waits=[wsg[1]])
            wsu = gu_pool.acquire()
            ev_wu = gdma(wsu[0][:, :], WU[ht, :, :], waits=[wsu[1]])
            pg = psum.acquire()
            lg = None
            for cc in range(CCN):
                w = []
                if cc == 0:
                    w = [ev_wg, r1bf_tqt[tqt], pg[1]]
                    if ev_down_done_prev is not None:
                        w.append(ev_down_done_prev)
                lg = mm(pg[0][:, :], wsg[0][:, cc * 128:(cc + 1) * 128],
                        r1_ap(cc, tqt), start=(cc == 0),
                        stop=(cc == CCN - 1), waits=w, inc=(cc == CCN - 1))
            pu = psum.acquire()
            lu = None
            for cc in range(CCN):
                w = [ev_wu, pu[1]] if cc == 0 else []
                lu = mm(pu[0][:, :], wsu[0][:, cc * 128:(cc + 1) * 128],
                        r1_ap(cc, tqt), start=(cc == 0),
                        stop=(cc == CCN - 1), waits=w, inc=(cc == CCN - 1))
            ga = wpool.acquire()
            ev_g = sact(ga[0][:, :], pg[0][:, :], AF.Silu, waits=[lg, ga[1]])
            psum.release(pg, ev_g)
            hsl = a_bf[:, hti * TQ + tqt * 512: hti * TQ + (tqt + 1) * 512]
            ev_h = vtt(hsl, pu[0][:, :], ga[0][:, :], ALU.mult, waits=[ev_g, lu])
            wpool.release(ga, ev_h)
            psum.release(pu, ev_h)
            hid_ev[(hti, tqt)] = ev_h
            gu_pool.release(wsg, lg)
            gu_pool.release(wsu, lu)

        def down(half, ot, tqts, ev_hid):
            ds = d_pool.acquire()
            ev_w = gdma(ds[0][:, :], WD[half * 16 + ot, :, :], waits=[ds[1]])
            ld = None
            for tqt in tqts:
                pd = psum.acquire()
                for hc in range(32):
                    w = []
                    if hc == 0:
                        w = [ev_w, ev_hid, pd[1]]
                    ld = mm(pd[0][:, :], ds[0][:, hc * 128:(hc + 1) * 128],
                            a_bf[:, hc * TQ + tqt * 512: hc * TQ + (tqt + 1) * 512],
                            start=(hc == 0), stop=(hc == 31),
                            waits=w, inc=(hc == 31))
                sl = s_f[:, ot * TQ + tqt * 512: ot * TQ + (tqt + 1) * 512]
                wv = [ld]
                if half == 0:
                    wv.append(ln1[(ot, tqt)][0])
                else:
                    wv.append(down_evac[(ot, tqt)])
                ev = vtt(sl, sl, pd[0][:, :], ALU.add, waits=wv)
                down_evac[(ot, tqt)] = ev
                psum.release(pd, ev)
            d_pool.release(ds, ld)

        hid_ev = {}
        for tqt in range(NTQ):
            for hti in range(32):
                gate_up(hti, tqt, hid_ev)
        ev_hid_all = hid_ev[(31, NTQ - 1)]
        for ot in range(NOT_):
            down(0, ot, range(NTQ), ev_hid_all)
        ev_down_done_prev = ("sT", sch.val("sT"))

        hid_ev = {}
        for tqt in range(NTQ):
            for hti in range(32):
                gate_up(32 + hti, tqt, hid_ev)
        ev_hid_all = hid_ev[(31, NTQ - 1)]
        ev_gu_last = ("sT", sch.val("sT"))
        d_pool.free.append([dbig_a, ev_gu_last])
        for tqt in range(NTQ):
            for ot in range(NOT_):
                down(1, ot, [tqt], ev_hid_all)
            layernorm_tqt(tqt, down_evac, False, ln2)
            for cc in range(CCN):
                sdma(OUT[:, cc * TQ + tqt * 512: cc * TQ + (tqt + 1) * 512],
                     s_f[:, cc * TQ + tqt * 512: cc * TQ + (tqt + 1) * 512],
                     waits=[ln2[(cc, tqt)][0]], sem="sout")
        sch.op("sync", lambda g: g.wait_ge(sem_h["sout"], sch.val("sout")),
               waits=())

        with nc.Block() as block:
            def emit_engine(eng_name, eng):
                guar = {}
                if eng_name in ("scalar", "vector", "tensor"):
                    eng.wait_ge(sem_h[ev_const[0]], ev_const[1])
                    guar[ev_const[0]] = ev_const[1]
                if eng_name == "vector":
                    eng.wait_ge(sem_h[ev_synth[0]], ev_synth[1])
                    guar[ev_synth[0]] = ev_synth[1]
                for (engine, emit, waits, inc) in sch.ops:
                    if engine != eng_name:
                        continue
                    for (sem, val) in waits:
                        if guar.get(sem, 0) >= val:
                            continue
                        eng.wait_ge(sem_h[sem], val)
                        guar[sem] = val
                    inst = emit(eng)
                    if inc is not None:
                        sem, amt = inc
                        if inst is not None:
                            inst.then_inc(sem_h[sem], amt)

            @block.sync
            def _(sy):
                emit_engine("sync", sy)

            @block.gpsimd
            def _(g):
                emit_engine("gpsimd", g)

            @block.tensor
            def _(t):
                emit_engine("tensor", t)

            @block.scalar
            def _(s):
                emit_engine("scalar", s)

            @block.vector
            def _(v):
                emit_engine("vector", v)

    return nc


PERM = {0: [0, 3, 1, 2], 1: [1, 2, 0, 3]}


def _images(x_b, par):
    perm = PERM[par]
    xt = x_b.reshape(4, 512, C)[perm].reshape(T, C)
    xkv = np.ascontiguousarray(
        xt.reshape(T, CCN, 128).transpose(2, 1, 0).reshape(128, CCN * KV))
    xq = np.ascontiguousarray(
        xt[:1024].reshape(TQ, CCN, 128).transpose(2, 1, 0).reshape(128, CCN * TQ))
    return xkv, xq


def _tiles_lhsT(w, ntile):
    return np.ascontiguousarray(
        w.reshape(ntile, 128, CCN, 128).transpose(0, 3, 2, 1)
        .reshape(ntile, 128, CCN * 128))


def kernel(L0_attn_x, L0_attn_Wq, L0_attn_Wk, L0_attn_Wv, L0_attn_Wo,
           L0_ffn_w_gate, L0_ffn_w_up, L0_ffn_w_down):
    x = np.asarray(L0_attn_x, dtype=np.float32)
    Wq = np.asarray(L0_attn_Wq, dtype=np.float32)
    Wk = np.asarray(L0_attn_Wk, dtype=np.float32)
    Wv = np.asarray(L0_attn_Wv, dtype=np.float32)
    Wo = np.asarray(L0_attn_Wo, dtype=np.float32)
    wg = np.asarray(L0_ffn_w_gate, dtype=np.float32)
    wu = np.asarray(L0_ffn_w_up, dtype=np.float32)
    wd = np.asarray(L0_ffn_w_down, dtype=np.float32)

    WKT = _tiles_lhsT(Wk, NH)
    WQT = _tiles_lhsT(Wq, NH)
    WOT = _tiles_lhsT(Wo, NOT_)
    WGi = _tiles_lhsT(wg, NHT)
    WUi = _tiles_lhsT(wu, NHT)
    WVT = np.ascontiguousarray(
        Wv.reshape(4, 512, CCN, 128).transpose(0, 3, 2, 1).reshape(4, 128, CCN * 512))
    WDi = np.ascontiguousarray(
        wd.reshape(NOT_, 128, 2, 32, 128).transpose(2, 0, 4, 3, 1)
        .reshape(32, 128, 32 * 128))

    iu = np.arange(896)[None, :]
    ip = np.arange(128)[:, None]
    RAMPi = (iu >= ip + 384).astype(BFD)
    ONESi = np.ones((128, 128), dtype=BFD)

    in_maps = []
    for core in range(8):
        b, par = core // 2, core % 2
        xkv, xq = _images(x[b], par)
        CONSTSi = np.zeros((128, NCONST), dtype=np.float32)
        for i, v in enumerate(CONST_VALS):
            CONSTSi[:, i] = v
        pm1 = (0.0, 0.0) if par == 0 else (1.0, C_B01)
        pm2 = (1.0, C_B01) if par == 0 else (0.0, 0.0)
        CONSTSi[:, 10], CONSTSi[:, 11] = pm1
        CONSTSi[:, 12], CONSTSi[:, 13] = pm2
        CONSTSi[:, 14] = 10.0
        CONSTSi[:, 15] = 128.0
        in_maps.append({
            "XKV": xkv, "XQ": xq, "WVT": WVT, "WKT": WKT, "WQT": WQT,
            "WOT": WOT, "WG": WGi, "WU": WUi, "WD": WDi,
            "RAMP": RAMPi, "ONESD": ONESi, "CONSTS": CONSTSi,
        })

    nc = build_nc()
    res = run_bass_kernel_spmd(nc, in_maps, core_ids=list(range(8)))

    out = np.empty((B, T, C), dtype=np.asarray(res.results[0]["OUT"]).dtype)
    for core in range(8):
        b, par = core // 2, core % 2
        o = res.results[core]["OUT"]
        loc = o.reshape(128, CCN, TQ).transpose(2, 1, 0).reshape(TQ, C)
        for tqt in range(2):
            pos = PERM[par][tqt]
            out[b, 512 * pos: 512 * pos + 512, :] = loc[512 * tqt: 512 * tqt + 512]
    return out
